# version 18
# baseline (speedup 1.0000x reference)
import numpy as np
import concourse.bass as bass
import concourse.mybir as mybir
from concourse.bass_utils import run_bass_kernel_spmd

F32 = mybir.dt.float32
BF16 = mybir.dt.bfloat16
I32 = mybir.dt.int32
AX = mybir.AxisListType.X
ALU = mybir.AluOpType
AF = mybir.ActivationFunctionType

DEBUG = False

D = 1024
S = 8192
OWN = 4096
H = 16
DH = 64
INTER = 4096
NT = 64
NTO = 32
NPAIR = 8
NIG = 8
NJC2 = 32
EPS = 1e-5
ENGS = ("sy", "po", "ve", "te", "sc")


class Builder:
    def __init__(self):
        self.nc = bass.Bass("TRN2", target_bir_lowering=False)
        self.steps = []
        self.cnt = {}
        self.semh = {}
        self.water = {}
        self.lastw = {}

    def mksem(self, stack, name):
        self.semh[name] = stack.enter_context(self.nc.semaphore(name))
        self.cnt[name] = 0

    def ensure(self, name):
        if name not in self.cnt:
            self.mksem(self._stack, name)

    def raw(self, eng, fn):
        self.lastw.pop(eng, None)
        self.steps.append((eng, fn))

    def wait(self, eng, sem, tgt=None):
        self.ensure(sem)
        tgt = self.cnt[sem] if tgt is None else tgt
        if tgt <= 0 or self.water.get((eng, sem), 0) >= tgt:
            return
        self.water[(eng, sem)] = tgt
        last = self.lastw.get(eng)
        if last is not None and last[0] == sem:
            last[1][0] = max(last[1][0], tgt)
            return
        h = self.semh[sem]
        box = [tgt]
        self.raw(eng, lambda e, h=h, b=box: e.wait_ge(h, b[0]))
        self.lastw[eng] = (sem, box)

    def wev(self, eng, ev):
        if ev is not None:
            self.wait(eng, ev[0], ev[1])

    def op(self, eng, build, inc=None, n=1):
        ev = None
        h = None
        if inc is not None:
            self.ensure(inc)
            self.cnt[inc] += n
            ev = (inc, self.cnt[inc])
            h = self.semh[inc]

        def f(e, build=build, h=h, n=n):
            ins = build(e)
            if h is not None:
                ins.then_inc(h, n)
        self.raw(eng, f)
        return ev

    def barrier(self):
        for eng in ENGS:
            for s in self.cnt:
                self.wait(eng, s)

    def replay(self):
        with self.nc.Block() as block:
            def run(eng_obj, name):
                for eng, fn in self.steps:
                    if eng == name:
                        fn(eng_obj)
            block.sync(lambda e: run(e, "sy"))
            block.gpsimd(lambda e: run(e, "po"))
            block.vector(lambda e: run(e, "ve"))
            block.tensor(lambda e: run(e, "te"))
            block.scalar(lambda e: run(e, "sc"))


def build_program():
    from contextlib import ExitStack
    B = Builder()
    nc = B.nc

    x = nc.dram_tensor("x", [S, D], F32, kind="ExternalInput").ap()
    w_q = nc.dram_tensor("w_q", [D, D], F32, kind="ExternalInput").ap()
    w_kv = nc.dram_tensor("w_kv", [D, 2 * D], F32, kind="ExternalInput").ap()
    w_out = nc.dram_tensor("w_out", [D, D], F32, kind="ExternalInput").ap()
    b_out = nc.dram_tensor("b_out", [1, D], F32, kind="ExternalInput").ap()
    scale = nc.dram_tensor("scale", [1, H], F32, kind="ExternalInput").ap()
    W1 = nc.dram_tensor("W1", [D, INTER], F32, kind="ExternalInput").ap()
    b1 = nc.dram_tensor("b1", [32, 128], F32, kind="ExternalInput").ap()
    W2 = nc.dram_tensor("W2", [INTER, D], F32, kind="ExternalInput").ap()
    b2 = nc.dram_tensor("b2", [1, D], F32, kind="ExternalInput").ap()
    y = nc.dram_tensor("y", [OWN, D], F32, kind="ExternalOutput").ap()
    scr_kind = "ExternalOutput" if DEBUG else "Internal"
    kT_sp = nc.dram_tensor("kT_sp", [D, S], BF16, kind=scr_kind).ap()
    v_sp = nc.dram_tensor("v_sp", [NPAIR, NT, 128, 130], BF16, kind=scr_kind).ap()
    qT_sp = nc.dram_tensor("qT_sp", [D, OWN], BF16, kind=scr_kind).ap()
    o_sp = nc.dram_tensor("o_sp", [D, OWN], BF16, kind=scr_kind).ap()
    W1bf = nc.dram_tensor("W1bf", [D, INTER], BF16, kind="Internal").ap()
    W2bf = nc.dram_tensor("W2bf", [INTER, D], BF16, kind="Internal").ap()
    wobf = nc.dram_tensor("wobf", [D, D], BF16, kind="Internal").ap()

    stack = ExitStack()
    with stack:
        sb = lambda n, s, d: stack.enter_context(nc.sbuf_tensor(n, s, d))

        A2 = sb("A2", [128, 69632], BF16)
        A4 = sb("A4", [128, 12288], F32)
        sscale = sb("sscale", [1, H], F32)
        onesf = sb("onesf", [1, 128], F32)
        ones64 = sb("ones64", [1, 64], F32)
        sb1 = sb("sb1", [32, 128], F32)
        sbo = sb("sbo", [1, D], F32)
        sb2 = sb("sb2", [1, D], F32)
        io32 = sb("io32", [128, 128], I32)
        identb = sb("identb", [128, 128], BF16)
        identf = sb("identf", [32, 32], F32)
        exp_sc = sb("exp_sc", [128, H], F32)
        negM = sb("negM", [128, H], F32)
        b1c = sb("b1c", [128, 32], F32)
        borep = sb("borep", [128, D], F32)
        b2rep = sb("b2rep", [128, D], F32)
        rl = sb("rl", [1, 1024], F32)
        c_su = sb("c_su", [128, 2], F32)
        c_mu = sb("c_mu", [128, 2], F32)
        c_mm = sb("c_mm", [128, 2], F32)
        c_ssq = sb("c_ssq", [128, 2], F32)
        c_v1 = sb("c_v1", [128, 2], F32)
        c_var = sb("c_var", [128, 2], F32)
        c_lnv = sb("c_lnv", [128, 2], F32)
        c_rstd = sb("c_rstd", [128, 2], F32)
        c_ssk = sb("c_ssk", [128, 2 * H], F32)
        c_ssk2 = sb("c_ssk2", [128, 2 * H], F32)
        c_t1k = sb("c_t1k", [128, 2 * H], F32)
        c_rsk = sb("c_rsk", [128, 2 * H], F32)
        c_rsq = sb("c_rsq", [128, 2 * H], F32)
        c_eps = sb("c_eps", [128, 1], F32)
        c_bns = sb("c_bns", [128, 24], F32)
        c_mv = sb("c_mv", [128, 4], F32)

        psf_h = stack.enter_context(nc.psum_tensor("PSF", [128, 4096], F32))
        PSF = psf_h[:]
        PSB = PSF[:, 3584:4096].bitcast(BF16)
        PSB2 = PSF[:, 3072:3584].bitcast(BF16)

        B._stack = stack
        for s in ("SQ", "SP", "SA", "SV"):
            B.mksem(stack, s)

        def a2(off, n):
            return A2[:, off:off + n]

        def a4(off, n):
            return A4[:, off:off + n]

        ev_sc_ld = B.op("sy", lambda e: e.dma_start(out=sscale[:], in_=scale), "Dsc", 16)
        ev_b1_ld = B.op("sy", lambda e: e.dma_start(out=sb1[:], in_=b1), "Db1", 16)
        ev_bo_ld = B.op("sy", lambda e: e.dma_start(out=sbo[:], in_=b_out), "Dbo", 16)
        ev_b2_ld = B.op("sy", lambda e: e.dma_start(out=sb2[:], in_=b2), "Db2", 16)
        B.op("po", lambda e: e.iota(io32[:], pattern=[[1, 128]], base=0, channel_multiplier=-1), "SQ")
        B.wait("po", "SQ")
        ev_idb = B.op("po", lambda e: e.tensor_single_scalar(identb[:], io32[:], 0, op=ALU.is_equal), "SQ")
        ev_idf = B.op("po", lambda e: e.tensor_single_scalar(identf[:], io32[0:32, 0:32], 0, op=ALU.is_equal), "SQ")
        ev_ones = B.op("po", lambda e: e.memset(onesf[:], 1.0), "SQ")
        ev_o64 = B.op("po", lambda e: e.memset(ones64[:], 1.0), "SQ")
        ev_eps = B.op("po", lambda e: e.memset(c_eps[:], EPS), "SQ")

        B.wev("te", ev_sc_ld)
        B.wev("te", ev_ones)
        ev = B.op("te", lambda e: e.matmul(PSF[:, 0:16], lhsT=onesf[:], rhs=sscale[:],
                                           start=True, stop=True), "SP")
        B.wev("sc", ev)
        ev = B.op("sc", lambda e: e.activation(exp_sc[:], PSF[:, 0:16], AF.Exp), "SA")
        B.wev("ve", ev)
        ev_negm = B.op("ve", lambda e: e.tensor_scalar_mul(negM[:], exp_sc[:], -1.0), "SV")
        B.wev("te", ev_bo_ld)
        B.op("te", lambda e: e.matmul(PSF[:, 512:1024], lhsT=onesf[:], rhs=sbo[0:1, 0:512],
                                      start=True, stop=True), "SP")
        B.op("te", lambda e: e.matmul(PSF[:, 1024:1536], lhsT=onesf[:], rhs=sbo[0:1, 512:1024],
                                      start=True, stop=True), "SP")
        B.wev("te", ev_b2_ld)
        B.op("te", lambda e: e.matmul(PSF[:, 1536:2048], lhsT=onesf[:], rhs=sb2[0:1, 0:512],
                                      start=True, stop=True), "SP")
        B.op("te", lambda e: e.matmul(PSF[:, 2048:2560], lhsT=onesf[:], rhs=sb2[0:1, 512:1024],
                                      start=True, stop=True), "SP")
        B.wev("te", ev_b1_ld)
        B.wev("te", ev_idf)
        ev = B.op("te", lambda e: e.transpose(PSF[:, 2560:2592], sb1[:], identf[:]), "SP")
        B.wev("ve", ev)
        B.op("ve", lambda e: e.tensor_copy(borep[:], PSF[:, 512:1536]), "SV")
        B.op("ve", lambda e: e.tensor_copy(b2rep[:], PSF[:, 1536:2560]), "SV")
        ev_bconsts = B.op("ve", lambda e: e.tensor_copy(b1c[:], PSF[:, 2560:2592]), "SV")
        B.barrier()

        o_wkvq = 0
        o_xn16 = 24576
        o_xnT = 26624
        o_k16 = 28672
        o_q16 = 30720
        o_vst = 32768
        o_kTt = 34848
        o_qTt = 36896
        o_wc16 = 38944
        f_xst = 0
        f_lnsq = 4096
        f_kev = 5120
        f_qev = 6144
        f_wst = 7168

        f_lnsq2 = 11264

        def emit_ln(xin_ev, xin_ap, out_ap, cs=0):
            bns = c_bns[:, cs * 12:(cs + 1) * 12]
            mv = c_mv[:, cs * 2:(cs + 1) * 2]
            lnv = c_lnv[:, cs:cs + 1]
            rstd = c_rstd[:, cs:cs + 1]
            B.wev("ve", xin_ev)
            B.op("ve", lambda e: e.bn_stats(bns[:, 0:6], xin_ap[:, 0:512]), "SV")
            e2 = B.op("ve", lambda e: e.bn_stats(bns[:, 6:12], xin_ap[:, 512:1024]), "SV")
            B.wev("ve", e2)
            e3 = B.op("ve", lambda e: e.bn_aggr(mv, bns), "SV")
            B.wev("sc", e3)
            e8 = B.op("sc", lambda e: e.activation(lnv, mv[:, 1:2], AF.Ln, bias=c_eps[:, 0:1]), "SA")
            B.wev("sc", e8)
            e9 = B.op("sc", lambda e: e.activation(rstd, lnv, AF.Exp, scale=-0.5), "SA")
            B.wev("ve", e9)
            return B.op("ve", lambda e: e.tensor_scalar(out_ap, xin_ap, mv[:, 0:1], rstd,
                                                        op0=ALU.subtract, op1=ALU.mult), "SV")

        wst_free = [None, None]
        for c in range(8):
            p = c % 2
            wst = a4(f_wst + p * 2048, 2048)
            B.wev("sy", wst_free[p])
            ld1 = B.op("sy", lambda e, c=c, wst=wst: e.dma_start(
                out=wst, in_=w_kv[c * 128:(c + 1) * 128, :]), "Dw%d" % p, 16)
            B.wev("ve", ld1)
            cp1 = B.op("ve", lambda e, c=c, wst=wst: e.tensor_copy(
                a2(o_wkvq + c * 3072, 2048), wst), "SV")
            B.wev("sy", cp1)
            ld2 = B.op("sy", lambda e, c=c, wst=wst: e.dma_start(
                out=wst[:, 0:1024], in_=w_q[c * 128:(c + 1) * 128, :]), "Dw%d" % p, 16)
            B.wev("ve", ld2)
            wst_free[p] = B.op("ve", lambda e, c=c, wst=wst: e.tensor_copy(
                a2(o_wkvq + c * 3072 + 2048, 1024), wst[:, 0:1024]), "SV")
        ev_wkvq = wst_free[1]

        units = []
        for c in range(8):
            for hhalf in range(2):
                units.append((W1[c * 128:(c + 1) * 128, hhalf * 2048:(hhalf + 1) * 2048],
                              W1bf[c * 128:(c + 1) * 128, hhalf * 2048:(hhalf + 1) * 2048], 2048))
        for ic in range(32):
            units.append((W2[ic * 128:(ic + 1) * 128, :],
                          W2bf[ic * 128:(ic + 1) * 128, :], 1024))
        for c in range(8):
            units.append((w_out[c * 128:(c + 1) * 128, :],
                          wobf[c * 128:(c + 1) * 128, :], 1024))
        unit_i = [0]
        wc_free = [None, None]
        wst2_free = [wst_free[0], wst_free[1]]

        def emit_wunit():
            if unit_i[0] >= len(units):
                return
            i = unit_i[0]
            unit_i[0] += 1
            src, dst, ncol = units[i]
            p = i % 2
            wst = a4(f_wst + p * 2048, ncol)
            wc = a2(o_wc16 + p * 2048, ncol)
            B.wev("sy", wst2_free[p])
            ld = B.op("sy", lambda e, src=src, wst=wst: e.dma_start(out=wst, in_=src), "Dw%d" % p, 16)
            B.wev("ve", ld)
            B.wev("ve", wc_free[p])
            cp = B.op("ve", lambda e, wst=wst, wc=wc: e.tensor_copy(wc, wst), "SV")
            wst2_free[p] = cp
            B.wev("po", cp)
            wc_free[p] = B.op("po", lambda e, dst=dst, wc=wc: e.dma_start(out=dst, in_=wc), "Dwc%d" % p, 16)

        for p in range(2):
            vst3 = a2(o_vst + p * 1040, 1040).rearrange("p (s c) -> p s c", c=65)
            B.op("po", lambda e, vst3=vst3: e.memset(vst3[:, :, 64:65], 1.0), "SQ")
        ev_vones = ("SQ", B.cnt["SQ"])

        x_free = [None, None, None, None]
        x_ld = [None] * NT

        def emit_xload(t):
            p = t % 4
            B.wev("sy", x_free[p])
            x_ld[t] = B.op("sy", lambda e, t=t, p=p: e.dma_start(
                out=a4(f_xst + p * 1024, D), in_=x[t * 128:(t + 1) * 128, :]), "Dx%d" % p, 16)

        emit_xload(0)
        emit_xload(1)

        pg = PSF[:, 0:3072]
        pg_free = [None]
        psb_free = [None]
        psb2_free = [None]
        kTt_free = [None, None]
        qTt_free = [None, None]
        vst_free = [None, None]
        k16_free = [None, None]
        q16_free = [None, None]
        xn16_free = [None, None]
        xnT_free = [None, None]
        kev_free = [None]
        qev_free = [None]

        st1 = {}

        def emit_S1a(t):
            own = t < NTO
            if t + 2 < NT:
                emit_xload(t + 2)
            emit_wunit()
            p2 = t % 2
            cs = t % 2
            xst = a4(f_xst + (t % 4) * 1024, D)
            xn16 = a2(o_xn16 + p2 * 1024, D)
            xnT = a2(o_xnT + p2 * 1024, D)

            B.wev("ve", xn16_free[p2])
            ev_xn = emit_ln(x_ld[t], xst, xn16, cs)
            x_free[t % 4] = ev_xn

            B.wev("te", ev_xn)
            B.wev("te", ev_idb)
            B.wev("te", psb2_free[0])
            for c in range(8):
                evt = B.op("te", lambda e, c=c, xn16=xn16: e.transpose(
                    PSB2[:, c * 128:(c + 1) * 128], xn16[:, c * 128:(c + 1) * 128], identb[:]),
                    "SP" if c == 7 else None)
            B.wev("sc", evt)
            B.wev("sc", xnT_free[p2])
            B.op("sc", lambda e, xnT=xnT: e.copy(xnT[:, 0:512], PSB2[:, 0:512]), "SA")
            ev_xnT = B.op("sc", lambda e, xnT=xnT: e.copy(xnT[:, 512:1024], PSB2[:, 512:1024]), "SA")
            psb2_free[0] = ev_xnT
            xn16_free[p2] = ev_xnT
            st1[t] = [ev_xnT, cs, own, p2, None]

        def emit_S1b(t):
            ev_xnT, cs, own, p2, _ = st1[t]
            xnT = a2(o_xnT + p2 * 1024, D)
            B.wev("te", ev_xnT)
            if t == 0:
                B.wev("te", ev_wkvq)
            if pg_free[0] is not None:
                for _ev in pg_free[0]:
                    B.wev("te", _ev)
            nsl = 6 if own else 4
            for ns in range(nsl):
                for c in range(8):
                    evp = B.op("te", lambda e, ns=ns, c=c, xnT=xnT: e.matmul(
                        pg[:, ns * 512:(ns + 1) * 512],
                        lhsT=xnT[:, c * 128:(c + 1) * 128],
                        rhs=a2(o_wkvq + c * 3072 + ns * 512, 512),
                        start=(c == 0), stop=(c == 7)),
                        "SP" if (ns == nsl - 1 and c == 7) else None)
            st1[t][4] = evp
            xnT_free[p2] = evp

        def emit_S2(t):
            _xnT, cs, own, p2, ev_pg = st1.pop(t)
            kev = a4(f_kev, D)
            B.wev("sc", ev_pg)
            B.wev("sc", kev_free[0])
            B.wev("sc", qev_free[0])
            e_k0 = B.op("sc", lambda e: e.copy(kev, pg[:, 0:1024]), "SA")
            if own:
                qev = a4(f_qev, D)
                e_q0 = B.op("sc", lambda e: e.copy(qev, pg[:, 2048:3072]), "SA")
            vst = a2(o_vst + p2 * 1040, 1040)
            B.wev("ve", ev_pg)
            B.wev("ve", vst_free[p2])
            if t < 2:
                B.wev("ve", ev_vones)
            for pr in range(8):
                e_v = B.op("ve", lambda e, pr=pr, vst=vst: e.tensor_copy(
                    vst.rearrange("p (s c) -> p s c", c=65)[:, 2 * pr:2 * pr + 2, 0:64],
                    pg[:, 1024 + pr * 128:1024 + (pr + 1) * 128].rearrange("p (s d) -> p s d", d=DH)),
                    "SV" if pr == 7 else None)
            ev_vdone = e_v
            pg_free[0] = [ev_vdone, e_q0] if own else [ev_vdone, e_k0]

            B.wev("sc", e_k0)
            ksq = a4(f_lnsq if cs == 0 else f_lnsq2, D)
            kssk = c_ssk[:, cs * H:(cs + 1) * H]
            kssk2 = c_ssk2[:, cs * H:(cs + 1) * H]
            kt1 = c_t1k[:, cs * H:(cs + 1) * H]
            krs = c_rsk[:, cs * H:(cs + 1) * H]
            e_k1 = B.op("sc", lambda e, ksq=ksq: e.activation(ksq, kev, AF.Square), "SA")
            B.wev("ve", e_k1)
            e_k2 = B.op("ve", lambda e, ksq=ksq, kssk=kssk: e.reduce_sum(
                kssk, ksq.rearrange("p (s d) -> p s d", d=DH), axis=AX), "SV")
            B.wev("ve", e_k2)
            e_k3 = B.op("ve", lambda e, kssk=kssk, kssk2=kssk2: e.tensor_scalar_max(kssk2, kssk, 1e-24), "SV")
            B.wev("sc", e_k3)
            e_k4 = B.op("sc", lambda e, kssk2=kssk2, kt1=kt1: e.activation(kt1, kssk2, AF.Ln), "SA")
            B.wev("sc", e_k4)
            e_k5 = B.op("sc", lambda e, kt1=kt1, krs=krs: e.activation(krs, kt1, AF.Exp, scale=-0.5), "SA")
            k16 = a2(o_k16 + p2 * 1024, D)
            B.wev("po", e_k5)
            B.wev("po", k16_free[p2])
            e_k6 = B.op("po", lambda e, k16=k16, krs=krs: e.tensor_tensor(
                k16.rearrange("p (s d) -> p s d", d=DH),
                kev.rearrange("p (s d) -> p s d", d=DH),
                krs.unsqueeze(2).broadcast_to([128, H, DH]), op=ALU.mult), "SQ")
            kev_free[0] = e_k6

            if own:
                B.wev("sc", e_q0)
                qsq = a4(f_lnsq if cs == 0 else f_lnsq2, D)
                qrs = c_rsq[:, cs * H:(cs + 1) * H]
                e_q1 = B.op("sc", lambda e, qsq=qsq: e.activation(qsq, qev, AF.Square), "SA")
                B.wev("ve", e_q1)
                e_q2 = B.op("ve", lambda e, qsq=qsq, kssk=kssk: e.reduce_sum(
                    kssk, qsq.rearrange("p (s d) -> p s d", d=DH), axis=AX), "SV")
                B.wev("ve", e_q2)
                e_q3 = B.op("ve", lambda e, kssk=kssk, kssk2=kssk2: e.tensor_scalar_max(kssk2, kssk, 1e-24), "SV")
                B.wev("sc", e_q3)
                e_q4 = B.op("sc", lambda e, kssk2=kssk2, kt1=kt1: e.activation(kt1, kssk2, AF.Ln), "SA")
                B.wev("sc", e_q4)
                e_q5 = B.op("sc", lambda e, kt1=kt1, qrs=qrs: e.activation(qrs, kt1, AF.Exp, scale=-0.5), "SA")
                B.wev("ve", e_q5)
                B.wev("ve", ev_negm)
                e_q6 = B.op("ve", lambda e, qrs=qrs: e.tensor_mul(qrs, qrs, exp_sc[:]), "SV")
                q16 = a2(o_q16 + p2 * 1024, D)
                B.wev("po", e_q6)
                B.wev("po", q16_free[p2])
                e_q7 = B.op("po", lambda e, q16=q16, qrs=qrs: e.tensor_tensor(
                    q16.rearrange("p (s d) -> p s d", d=DH),
                    qev.rearrange("p (s d) -> p s d", d=DH),
                    qrs.unsqueeze(2).broadcast_to([128, H, DH]), op=ALU.mult), "SQ")
                qev_free[0] = e_q7

            B.wev("te", e_k6)
            B.wev("te", psb_free[0])
            for c in range(8):
                evt = B.op("te", lambda e, c=c, k16=k16: e.transpose(
                    PSB[:, c * 128:(c + 1) * 128], k16[:, c * 128:(c + 1) * 128], identb[:]),
                    "SP" if c == 7 else None)
            kTt = a2(o_kTt + p2 * 1024, D)
            B.wev("ve", evt)
            B.wev("ve", kTt_free[p2])
            B.op("ve", lambda e, kTt=kTt: e.tensor_copy(kTt[:, 0:512], PSB[:, 0:512]), "SV")
            e_kTt = B.op("ve", lambda e, kTt=kTt: e.tensor_copy(kTt[:, 512:1024], PSB[:, 512:1024]), "SV")
            psb_free[0] = e_kTt
            k16_free[p2] = e_kTt
            B.wev("po", e_kTt)
            kTt_free[p2] = B.op("po", lambda e, t=t, kTt=kTt: e.dma_start(
                out=kT_sp[:, t * 128:(t + 1) * 128].rearrange("(c p) f -> p c f", p=128),
                in_=kTt.rearrange("p (c f) -> p c f", c=8)), "Dkt%d" % p2, 16)

            if own:
                B.wev("te", e_q7)
                B.wev("te", psb_free[0])
                for c in range(8):
                    evt = B.op("te", lambda e, c=c, q16=q16: e.transpose(
                        PSB[:, c * 128:(c + 1) * 128], q16[:, c * 128:(c + 1) * 128], identb[:]),
                        "SP" if c == 7 else None)
                qTt = a2(o_qTt + p2 * 1024, D)
                B.wev("ve", evt)
                B.wev("ve", qTt_free[p2])
                B.op("ve", lambda e, qTt=qTt: e.tensor_copy(qTt[:, 0:512], PSB[:, 0:512]), "SV")
                e_qTt = B.op("ve", lambda e, qTt=qTt: e.tensor_copy(qTt[:, 512:1024], PSB[:, 512:1024]), "SV")
                psb_free[0] = e_qTt
                q16_free[p2] = e_qTt
                B.wev("po", e_qTt)
                qTt_free[p2] = B.op("po", lambda e, t=t, qTt=qTt: e.dma_start(
                    out=qT_sp[:, t * 128:(t + 1) * 128].rearrange("(c p) f -> p c f", p=128),
                    in_=qTt.rearrange("p (c f) -> p c f", c=8)), "Dqt%d" % p2, 16)

            B.wev("po", ev_vdone)
            vst_free[p2] = B.op("po", lambda e, t=t, vst=vst: e.dma_start(
                out=v_sp[:, t].rearrange("c p f -> p c f"),
                in_=vst.rearrange("p (c f) -> p c f", c=8)), "Dvs%d" % p2, 16)

        for i in range(NT + 2):
            if i < NT:
                emit_S1a(i)
            if 2 <= i:
                emit_S2(i - 2)
            if 1 <= i <= NT:
                emit_S1b(i - 1)

        while unit_i[0] < len(units):
            emit_wunit()

        B.barrier()
        o_kT = 0
        o_vp = 16384
        o_qTb = 33024
        o_P16 = 34048
        o_ot = 37120
        f_osb = 0

        o0 = PSF[:, 0:512]
        o1 = PSF[:, 512:1024]
        RING0 = 1024

        kld = [None, None]
        vld = [None, None]
        kT_free = [None, None]
        vp_free = [None, None]
        qld = [None, None]
        qT_free = [None, None]
        slot_free = [None] * 6
        p16_free = [None, None]
        o_free = [None]
        osb_free = [None, None]
        ot_free = [None, None]
        rl_free = [None, None]

        def emit_kvload(pr):
            p = pr % 2
            B.wev("sy", kT_free[p])
            kld[p] = B.op("sy", lambda e, pr=pr, p=p: e.dma_start(
                out=a2(o_kT + p * 8192, 8192),
                in_=kT_sp[pr * 128:(pr + 1) * 128, :]), "Dk%d" % p, 16)
            B.wev("sy", vp_free[p])
            vld[p] = B.op("sy", lambda e, pr=pr, p=p: e.dma_start(
                out=a2(o_vp + p * 8320, 8320).rearrange("p (c f) -> p c f", c=64),
                in_=v_sp[pr].rearrange("c p f -> p c f")), "Dv%d" % p, 16)

        emit_kvload(0)
        emit_kvload(1)
        pending_div = [None]

        chunks = [(jc, eh) for jc in range(64) for eh in (0, 1)]
        groups = [chunks[i:i + 3] for i in range(0, 126, 3)] + [chunks[126:128]]

        for pr in range(NPAIR):
            p = pr % 2
            kT = a2(o_kT + p * 8192, 8192)
            vp = a2(o_vp + p * 8320, 8320)
            last_pv_ev = None
            last_score_ev = None

            def emit_qld(ig2, pr=pr):
                pq2 = ig2 % 2
                B.wev("sy", qT_free[pq2])
                qld[pq2] = B.op("sy", lambda e, pr=pr, ig2=ig2, pq2=pq2: e.dma_start(
                    out=a2(o_qTb + pq2 * 512, 512),
                    in_=qT_sp[pr * 128:(pr + 1) * 128, ig2 * 512:(ig2 + 1) * 512]), "Dq%d" % pq2, 16)

            emit_qld(0)
            emit_qld(1)
            for ig in range(NIG):
                pq = ig % 2
                qTb = a2(o_qTb + pq * 512, 512)
                first_pv = [True]
                pend = []
                seq = list(enumerate(groups)) + [(None, None), (None, None)]
                for gi, grp in seq:
                    if gi is not None:
                        gslot = (3 * gi) % 6
                        gpar = gi % 2
                        fd = 512 * len(grp)
                        for ci in range(len(grp)):
                            B.wev("te", slot_free[gslot + ci])
                        if gi == 0:
                            B.wev("te", kld[p])
                            B.wev("te", qld[pq])
                            if ig == 0:
                                B.wev("te", vld[p])
                        for ci, (jc, eh) in enumerate(grp):
                            sl = gslot + ci
                            evs = B.op("te", lambda e, eh=eh, jc=jc, sl=sl, kT=kT, qTb=qTb: e.matmul(
                                PSF[:, RING0 + sl * 512:RING0 + (sl + 1) * 512],
                                lhsT=kT[eh * 64:(eh + 1) * 64, jc * 128:(jc + 1) * 128],
                                rhs=qTb[eh * 64:(eh + 1) * 64, :],
                                start=True, stop=True, tile_position=(eh * 64, 0)),
                                "SP" if ci == len(grp) - 1 else None)
                        last_score_ev = evs
                    if len(pend) == 2 or (gi is None and pend):
                        pgrp, pgpar, peve = pend.pop(0)
                        if first_pv[0]:
                            B.wev("te", o_free[0])
                            first_pv[0] = False
                        B.wev("te", peve)
                        for ci, (jc, eh) in enumerate(pgrp):
                            oreg = o0 if eh == 0 else o1
                            evpv = B.op("te", lambda e, jc=jc, eh=eh, ci=ci, pgpar=pgpar, vp=vp, oreg=oreg: e.matmul(
                                oreg[0:65, :],
                                lhsT=vp[:, jc * 130 + eh * 65:jc * 130 + (eh + 1) * 65],
                                rhs=a2(o_P16 + pgpar * 1536 + ci * 512, 512),
                                start=(jc == 0), stop=(jc == 63)),
                                "SP" if ci == len(pgrp) - 1 else None)
                        p16_free[pgpar] = evpv
                        last_pv_ev = evpv
                    if gi is not None:
                        B.wev("sc", last_score_ev)
                        B.wev("sc", p16_free[gpar])
                        eve = B.op("sc", lambda e, gslot=gslot, gpar=gpar, fd=fd: e.activation(
                            a2(o_P16 + gpar * 1536, fd),
                            PSF[:, RING0 + gslot * 512:RING0 + gslot * 512 + fd], AF.Exp), "SA")
                        for ci in range(len(grp)):
                            slot_free[gslot + ci] = eve
                        pend.append((grp, gpar, eve))
                        if gi == 1 and pending_div[0] is not None:
                            pending_div[0]()
                            pending_div[0] = None
                last_pv_ev = evpv

                def emit_divide(pr=pr, ig=ig, last_pv_ev=last_pv_ev):
                    pp = (pr * NIG + ig) % 2
                    rlh = rl[0:1, 0:1024]
                    B.wev("ve", last_pv_ev)
                    B.wev("ve", osb_free[pp])
                    B.wev("ve", rl_free[0])
                    e_r0 = B.op("ve", lambda e, rlh=rlh: e.reciprocal(rlh[:, 0:512], PSF[64:65, 0:512]), "SV")
                    e_r1 = B.op("ve", lambda e, rlh=rlh: e.reciprocal(rlh[:, 512:1024], PSF[64:65, 512:1024]), "SV")
                    osb0 = a4(f_osb + pp * 1024, 512)
                    osb1 = a4(f_osb + pp * 1024 + 512, 512)
                    e_c0 = B.op("ve", lambda e, osb0=osb0: e.tensor_copy(osb0[0:64, :], PSF[0:64, 0:512]), "SV")
                    e_c1 = B.op("ve", lambda e, osb1=osb1: e.tensor_copy(osb1[0:64, :], PSF[0:64, 512:1024]), "SV")
                    B.wev("te", e_r0)
                    B.wev("te", e_c0)
                    B.wev("te", ev_o64)
                    e_b0 = B.op("te", lambda e, rlh=rlh: e.matmul(
                        PSF[0:64, 0:512], lhsT=ones64[:], rhs=rlh[:, 0:512], start=True, stop=True), "SP")
                    ot_a = a2(o_ot + pp * 1024, 512)
                    ot_b = a2(o_ot + pp * 1024 + 512, 512)
                    B.wev("ve", e_b0)
                    B.wev("ve", ot_free[pp])
                    e_m0 = B.op("ve", lambda e, osb0=osb0, ot_a=ot_a: e.tensor_tensor(
                        ot_a[0:64, :], osb0[0:64, :], PSF[0:64, 0:512], op=ALU.mult), "SV")
                    B.wev("te", e_r1)
                    B.wev("te", e_c1)
                    e_b1 = B.op("te", lambda e, rlh=rlh: e.matmul(
                        PSF[0:64, 512:1024], lhsT=ones64[:], rhs=rlh[:, 512:1024], start=True, stop=True), "SP")
                    rl_free[0] = e_b1
                    B.wev("ve", e_b1)
                    e_m1 = B.op("ve", lambda e, osb1=osb1, ot_b=ot_b: e.tensor_tensor(
                        ot_b[0:64, :], osb1[0:64, :], PSF[0:64, 512:1024], op=ALU.mult), "SV")
                    osb_free[pp] = e_m1
                    o_free[0] = e_m1
                    B.wev("po", e_m1)
                    B.op("po", lambda e, pr=pr, ig=ig, ot_a=ot_a: e.dma_start(
                        out=o_sp[pr * 128:pr * 128 + 64, ig * 512:(ig + 1) * 512], in_=ot_a[0:64, :]),
                        "Dosa%d" % pp, 16)
                    ot_free[pp] = B.op("po", lambda e, pr=pr, ig=ig, ot_b=ot_b: e.dma_start(
                        out=o_sp[pr * 128 + 64:pr * 128 + 128, ig * 512:(ig + 1) * 512], in_=ot_b[0:64, :]),
                        "Dosb%d" % pp, 16)
                pending_div[0] = emit_divide
                qT_free[pq] = last_score_ev
                if ig + 2 < NIG:
                    emit_qld(ig + 2)
            kT_free[p] = last_score_ev
            vp_free[p] = last_pv_ev
            if pr + 2 < NPAIR:
                emit_kvload(pr + 2)
        if pending_div[0] is not None:
            pending_div[0]()
            pending_div[0] = None

        B.barrier()
        o_wo = 0
        o_W2 = 8192
        o_W1r = 40960
        o_yt = 43008
        o_x1T = 59392
        o_otc = 67584
        f_x1g = 0
        f_s1 = 5120
        f_ln1 = 6144
        f_x3 = 7168
        f_out = 9216

        ev_wo = B.op("sy", lambda e: e.dma_start(
            out=a2(o_wo, 8192).rearrange("p (c f) -> p c f", c=8),
            in_=wobf.rearrange("(c p) f -> p c f", p=128)), "Dwo", 16)
        ev_W2 = B.op("sy", lambda e: e.dma_start(
            out=a2(o_W2, 32768).rearrange("p (c f) -> p c f", c=32),
            in_=W2bf.rearrange("(c p) f -> p c f", p=128)), "DW2", 16)

        pz = PSF[:, 0:1024]
        pz2 = PSF[:, 1024:2048]
        pyr = [PSF[:, 2048:2560], PSF[:, 2560:3072]]

        ot_ld = [None, None]
        otc_free = [None, None]
        x3_ld = [None, None]
        x3_free = [None, None]
        w1_ld = [None, None]
        w1_free = [None, None]
        py_free = [None, None]
        pz_free = None
        pz2_free = None
        yt_free = None
        x1T_free = None
        s1_free = None
        ln1_free = None
        out_free = [None, None]
        x1g_free = [None, None, None, None]
        last_y_dma = None

        def emit_ln3(xin_ev, xin_ap, out_ap, cs=0):
            return emit_ln(xin_ev, xin_ap, out_ap, cs)

        x1T_free2 = [None, None]
        for g in range(8):
            gp = g % 2
            x1T_base = o_x1T + gp * 4096
            ev_xt = [None] * 4
            for ti in range(4):
                t = g * 4 + ti
                pt2 = t % 2
                B.wev("sy", otc_free[pt2])
                ot_ld[pt2] = B.op("sy", lambda e, t=t, pt2=pt2: e.dma_start(
                    out=a2(o_otc + pt2 * 1024, 1024).rearrange("p (c f) -> p c f", c=8),
                    in_=o_sp[:, t * 128:(t + 1) * 128].rearrange("(c p) f -> p c f", p=128)),
                    "Dot%d" % pt2, 16)
                B.wev("sy", x3_free[pt2])
                x3_ld[pt2] = B.op("sy", lambda e, t=t, pt2=pt2: e.dma_start(
                    out=a4(f_x3 + pt2 * 1024, D), in_=x[t * 128:(t + 1) * 128, :]), "Dx3%d" % pt2, 16)

                B.wev("te", ot_ld[pt2])
                if t == 0:
                    B.wev("te", ev_wo)
                B.wev("te", pz_free)
                otc = a2(o_otc + pt2 * 1024, 1024)
                for ehh in range(2):
                    for c in range(8):
                        ev_z = B.op("te", lambda e, ehh=ehh, c=c, otc=otc: e.matmul(
                            pz[:, ehh * 512:(ehh + 1) * 512],
                            lhsT=otc[:, c * 128:(c + 1) * 128],
                            rhs=a2(o_wo + c * 1024 + ehh * 512, 512),
                            start=(c == 0), stop=(c == 7)),
                            "SP" if (ehh == 1 and c == 7) else None)
                otc_free[pt2] = ev_z
                B.wev("ve", ev_z)
                B.wev("ve", s1_free)
                B.wev("ve", ev_bconsts)
                e_s1 = B.op("ve", lambda e: e.tensor_add(a4(f_s1, D), pz, borep[:]), "SV")
                pz_free = e_s1
                B.wev("ve", ln1_free)
                e_l1 = emit_ln3(e_s1, a4(f_s1, D), a4(f_ln1, D), t % 2)
                s1_free = e_l1
                B.wev("ve", x3_ld[pt2])
                B.wev("ve", x1g_free[ti])
                e_x1 = B.op("ve", lambda e, ti=ti, pt2=pt2: e.tensor_add(
                    a4(f_x1g + ti * 1024, D), a4(f_ln1, D), a4(f_x3 + pt2 * 1024, D)), "SV")
                ln1_free = e_x1
                x3_free[pt2] = e_x1
                xnb = a2(o_x1T, 4096)
                e_ln2 = emit_ln3(e_x1, a4(f_x1g + ti * 1024, D), a2(o_otc + pt2 * 1024, 1024), t % 2)
                B.wev("te", e_ln2)
                B.wev("te", psb_free[0])
                for c in range(8):
                    evt = B.op("te", lambda e, c=c, pt2=pt2: e.transpose(
                        PSB[:, c * 128:(c + 1) * 128],
                        a2(o_otc + pt2 * 1024, 1024)[:, c * 128:(c + 1) * 128], identb[:]),
                        "SP" if c == 7 else None)
                otc_free[pt2] = evt
                B.wev("ve", evt)
                B.wev("ve", x1T_free2[gp])
                B.op("ve", lambda e, ti=ti, x1T_base=x1T_base: e.tensor_copy(
                    a2(x1T_base, 4096).rearrange("p (c f) -> p c f", c=8)[:, 0:4, ti * 128:(ti + 1) * 128],
                    PSB[:, 0:512].rearrange("p (c f) -> p c f", c=4)), "SV")
                ev_xt[ti] = B.op("ve", lambda e, ti=ti, x1T_base=x1T_base: e.tensor_copy(
                    a2(x1T_base, 4096).rearrange("p (c f) -> p c f", c=8)[:, 4:8, ti * 128:(ti + 1) * 128],
                    PSB[:, 512:1024].rearrange("p (c f) -> p c f", c=4)), "SV")
                psb_free[0] = ev_xt[ti]

            def emit_w1load(e_idx):
                pw = e_idx % 2
                B.wev("sy", w1_free[pw])
                w1_ld[pw] = B.op("sy", lambda e, e_idx=e_idx, pw=pw: e.dma_start(
                    out=a2(o_W1r + pw * 1024, 1024).rearrange("p (c f) -> p c f", c=8),
                    in_=W1bf[:, e_idx * 128:(e_idx + 1) * 128].rearrange("(c p) f -> p c f", p=128)),
                    "Dwr%d" % pw, 16)

            emit_w1load(0)
            emit_w1load(1)
            for ei in range(32):
                pw = ei % 2
                if ei + 2 < 32:
                    pass
                B.wev("te", w1_ld[pw])
                if ei < 2:
                    for tii in range(4):
                        B.wev("te", ev_xt[tii])
                B.wev("te", py_free[pw])
                if ei == 0:
                    B.wev("sc", yt_free)
                for c in range(8):
                    e_y = B.op("te", lambda e, ei=ei, c=c, pw=pw, x1T_base=x1T_base: e.matmul(
                        pyr[pw], lhsT=a2(o_W1r + pw * 1024, 1024)[:, c * 128:(c + 1) * 128],
                        rhs=a2(x1T_base + c * 512, 512),
                        start=(c == 0), stop=(c == 7)), "SP" if c == 7 else None)
                w1_free[pw] = e_y
                x1T_free2[gp] = e_y
                B.wev("sc", e_y)
                e_yt = B.op("sc", lambda e, ei=ei, pw=pw: e.activation(
                    a2(o_yt + ei * 512, 512), pyr[pw], AF.Relu, bias=b1c[:, ei:ei + 1]), "SA")
                py_free[pw] = e_yt
                if ei + 2 < 32:
                    emit_w1load(ei + 2)

            ev_yt_all = e_yt

            for ti in range(4):
                t = g * 4 + ti
                pt2 = t % 2
                B.wev("te", ev_yt_all)
                if g == 0 and ti == 0:
                    B.wev("te", ev_W2)
                B.wev("te", pz2_free)
                for ehh in range(2):
                    for ic in range(32):
                        e_z2 = B.op("te", lambda e, ehh=ehh, ic=ic, ti=ti: e.matmul(
                            pz2[:, ehh * 512:(ehh + 1) * 512],
                            lhsT=a2(o_yt + ic * 512, 512)[:, ti * 128:(ti + 1) * 128],
                            rhs=a2(o_W2 + ic * 1024 + ehh * 512, 512),
                            start=(ic == 0), stop=(ic == 31)),
                            "SP" if (ehh == 1 and ic == 31) else None)
                yt_free = e_z2
                B.wev("ve", e_z2)
                B.wev("ve", s1_free)
                e_s2 = B.op("ve", lambda e: e.tensor_add(a4(f_s1, D), pz2, b2rep[:]), "SV")
                pz2_free = e_s2
                B.wev("ve", ln1_free)
                e_l2 = emit_ln3(e_s2, a4(f_s1, D), a4(f_ln1, D), t % 2)
                s1_free = e_l2
                B.wev("ve", out_free[pt2])
                e_o = B.op("ve", lambda e, ti=ti, pt2=pt2: e.tensor_add(
                    a4(f_out + pt2 * 1024, D), a4(f_ln1, D), a4(f_x1g + ti * 1024, D)), "SV")
                ln1_free = e_o
                x1g_free[ti] = e_o
                B.wev("po", e_o)
                last_y_dma = B.op("po", lambda e, t=t, pt2=pt2: e.dma_start(
                    out=y[t * 128:(t + 1) * 128, :], in_=a4(f_out + pt2 * 1024, D)), "Dy%d" % pt2, 16)
                out_free[pt2] = last_y_dma

        B.wait("po", "Dy0")
        B.wait("po", "Dy1")
        B.replay()
    return B.nc


_NC_CACHE = []


def kernel(L0_attn_x, L0_attn_w_q, L0_attn_w_kv, L0_attn_w_out, L0_attn_b_out,
           L0_attn_scale, L0_ffn_W1, L0_ffn_b1, L0_ffn_W2, L0_ffn_b2):
    x = np.ascontiguousarray(np.asarray(L0_attn_x, dtype=np.float32))
    w_q = np.ascontiguousarray(np.asarray(L0_attn_w_q, dtype=np.float32))
    w_kv = np.ascontiguousarray(np.asarray(L0_attn_w_kv, dtype=np.float32))
    w_out = np.ascontiguousarray(np.asarray(L0_attn_w_out, dtype=np.float32))
    b_out = np.ascontiguousarray(np.asarray(L0_attn_b_out, dtype=np.float32)).reshape(1, D)
    sc = np.ascontiguousarray(np.asarray(L0_attn_scale, dtype=np.float32)).reshape(1, H)
    W1a = np.ascontiguousarray(np.asarray(L0_ffn_W1, dtype=np.float32))
    b1a = np.ascontiguousarray(np.asarray(L0_ffn_b1, dtype=np.float32)).reshape(32, 128)
    W2a = np.ascontiguousarray(np.asarray(L0_ffn_W2, dtype=np.float32))
    b2a = np.ascontiguousarray(np.asarray(L0_ffn_b2, dtype=np.float32)).reshape(1, D)

    if not _NC_CACHE:
        _NC_CACHE.append(build_program())
    nc = _NC_CACHE[0]

    in_maps = []
    for c in range(8):
        b, oh = c // 2, c % 2
        xb = x[b]
        if oh == 1:
            xc = np.ascontiguousarray(np.concatenate([xb[OWN:], xb[:OWN]], axis=0))
        else:
            xc = xb
        in_maps.append({
            "x": xc, "w_q": w_q, "w_kv": w_kv, "w_out": w_out, "b_out": b_out,
            "scale": sc, "W1": W1a, "b1": b1a, "W2": W2a, "b2": b2a,
        })
    res = run_bass_kernel_spmd(nc, in_maps, core_ids=list(range(8)))
    out = np.empty((4, S, D), dtype=np.float32)
    for c in range(8):
        b, oh = c // 2, c % 2
        out[b, oh * OWN:(oh + 1) * OWN] = res.results[c]["y"]
    return out


if __name__ == "__main__":
    rng = np.random.default_rng(0)
    inputs = {
        "L0_attn_x": rng.standard_normal((4, S, D), dtype=np.float32),
        "L0_attn_w_q": rng.standard_normal((D, D), dtype=np.float32) * 0.02,
        "L0_attn_w_kv": rng.standard_normal((D, 2 * D), dtype=np.float32) * 0.02,
        "L0_attn_w_out": rng.standard_normal((D, D), dtype=np.float32) * 0.02,
        "L0_attn_b_out": rng.standard_normal((D,), dtype=np.float32) * 0.1,
        "L0_attn_scale": rng.standard_normal((1, H, 1, 1), dtype=np.float32) - 3.0,
        "L0_ffn_W1": rng.standard_normal((D, INTER), dtype=np.float32) * 0.03,
        "L0_ffn_b1": rng.standard_normal((INTER,), dtype=np.float32) * 0.02,
        "L0_ffn_W2": rng.standard_normal((INTER, D), dtype=np.float32) * 0.015,
        "L0_ffn_b2": rng.standard_normal((D,), dtype=np.float32) * 0.02,
    }
    out = kernel(**inputs)
    print("out", out.shape, out.dtype, float(np.abs(out).max()))


# version 19 (final)
# speedup vs baseline: 1.1821x; 1.1821x over previous
import numpy as np
import concourse.bass as bass
import concourse.mybir as mybir
from concourse.bass_utils import run_bass_kernel_spmd

F32 = mybir.dt.float32
BF16 = mybir.dt.bfloat16
I32 = mybir.dt.int32
AX = mybir.AxisListType.X
ALU = mybir.AluOpType
AF = mybir.ActivationFunctionType

DEBUG = False

D = 1024
S = 8192
OWN = 4096
H = 16
DH = 64
INTER = 4096
NT = 64
NTO = 32
NPAIR = 8
NIG = 8
NJC2 = 32
EPS = 1e-5
ENGS = ("sy", "po", "ve", "te", "sc")


class Builder:
    def __init__(self):
        self.nc = bass.Bass("TRN2", target_bir_lowering=False)
        self.steps = []
        self.cnt = {}
        self.semh = {}
        self.water = {}
        self.lastw = {}

    def mksem(self, stack, name):
        self.semh[name] = stack.enter_context(self.nc.semaphore(name))
        self.cnt[name] = 0

    def ensure(self, name):
        if name not in self.cnt:
            self.mksem(self._stack, name)

    def raw(self, eng, fn):
        self.lastw.pop(eng, None)
        self.steps.append((eng, fn))

    def wait(self, eng, sem, tgt=None):
        self.ensure(sem)
        tgt = self.cnt[sem] if tgt is None else tgt
        if tgt <= 0 or self.water.get((eng, sem), 0) >= tgt:
            return
        self.water[(eng, sem)] = tgt
        last = self.lastw.get(eng)
        if last is not None and last[0] == sem:
            last[1][0] = max(last[1][0], tgt)
            return
        h = self.semh[sem]
        box = [tgt]
        self.raw(eng, lambda e, h=h, b=box: e.wait_ge(h, b[0]))
        self.lastw[eng] = (sem, box)

    def wev(self, eng, ev):
        if ev is not None:
            self.wait(eng, ev[0], ev[1])

    def op(self, eng, build, inc=None, n=1):
        ev = None
        h = None
        if inc is not None:
            self.ensure(inc)
            self.cnt[inc] += n
            ev = (inc, self.cnt[inc])
            h = self.semh[inc]

        def f(e, build=build, h=h, n=n):
            ins = build(e)
            if h is not None:
                ins.then_inc(h, n)
        self.raw(eng, f)
        return ev

    def barrier(self):
        for eng in ENGS:
            for s in self.cnt:
                self.wait(eng, s)

    def replay(self):
        with self.nc.Block() as block:
            def run(eng_obj, name):
                for eng, fn in self.steps:
                    if eng == name:
                        fn(eng_obj)
            block.sync(lambda e: run(e, "sy"))
            block.gpsimd(lambda e: run(e, "po"))
            block.vector(lambda e: run(e, "ve"))
            block.tensor(lambda e: run(e, "te"))
            block.scalar(lambda e: run(e, "sc"))


def build_program():
    from contextlib import ExitStack
    B = Builder()
    nc = B.nc

    x = nc.dram_tensor("x", [S, D], F32, kind="ExternalInput").ap()
    w_q = nc.dram_tensor("w_q", [D, D], F32, kind="ExternalInput").ap()
    w_kv = nc.dram_tensor("w_kv", [D, 2 * D], F32, kind="ExternalInput").ap()
    w_out = nc.dram_tensor("w_out", [D, D], F32, kind="ExternalInput").ap()
    b_out = nc.dram_tensor("b_out", [1, D], F32, kind="ExternalInput").ap()
    scale = nc.dram_tensor("scale", [1, H], F32, kind="ExternalInput").ap()
    W1 = nc.dram_tensor("W1", [D, INTER], F32, kind="ExternalInput").ap()
    b1 = nc.dram_tensor("b1", [32, 128], F32, kind="ExternalInput").ap()
    W2 = nc.dram_tensor("W2", [INTER, D], F32, kind="ExternalInput").ap()
    b2 = nc.dram_tensor("b2", [1, D], F32, kind="ExternalInput").ap()
    y = nc.dram_tensor("y", [OWN, D], F32, kind="ExternalOutput").ap()
    scr_kind = "ExternalOutput" if DEBUG else "Internal"
    kT_sp = nc.dram_tensor("kT_sp", [D, S], BF16, kind=scr_kind).ap()
    v_sp = nc.dram_tensor("v_sp", [NPAIR, NT, 128, 130], BF16, kind=scr_kind).ap()
    qT_sp = nc.dram_tensor("qT_sp", [D, OWN], BF16, kind=scr_kind).ap()
    o_sp = nc.dram_tensor("o_sp", [D, OWN], BF16, kind=scr_kind).ap()
    W1bf = nc.dram_tensor("W1bf", [D, INTER], BF16, kind="Internal").ap()
    W2bf = nc.dram_tensor("W2bf", [INTER, D], BF16, kind="Internal").ap()
    wobf = nc.dram_tensor("wobf", [D, D], BF16, kind="Internal").ap()

    stack = ExitStack()
    with stack:
        sb = lambda n, s, d: stack.enter_context(nc.sbuf_tensor(n, s, d))

        A2 = sb("A2", [128, 69632], BF16)
        A4 = sb("A4", [128, 12288], F32)
        sscale = sb("sscale", [1, H], F32)
        onesf = sb("onesf", [1, 128], F32)
        ones64 = sb("ones64", [1, 64], F32)
        sb1 = sb("sb1", [32, 128], F32)
        sbo = sb("sbo", [1, D], F32)
        sb2 = sb("sb2", [1, D], F32)
        io32 = sb("io32", [128, 128], I32)
        identb = sb("identb", [128, 128], BF16)
        identf = sb("identf", [32, 32], F32)
        exp_sc = sb("exp_sc", [128, H], F32)
        negM = sb("negM", [128, H], F32)
        b1c = sb("b1c", [128, 32], F32)
        borep = sb("borep", [128, D], F32)
        b2rep = sb("b2rep", [128, D], F32)
        rl = sb("rl", [1, 1024], F32)
        c_su = sb("c_su", [128, 2], F32)
        c_mu = sb("c_mu", [128, 2], F32)
        c_mm = sb("c_mm", [128, 2], F32)
        c_ssq = sb("c_ssq", [128, 2], F32)
        c_v1 = sb("c_v1", [128, 2], F32)
        c_var = sb("c_var", [128, 2], F32)
        c_lnv = sb("c_lnv", [128, 2], F32)
        c_rstd = sb("c_rstd", [128, 2], F32)
        c_ssk = sb("c_ssk", [128, 2 * H], F32)
        c_ssk2 = sb("c_ssk2", [128, 2 * H], F32)
        c_t1k = sb("c_t1k", [128, 2 * H], F32)
        c_rsk = sb("c_rsk", [128, 2 * H], F32)
        c_rsq = sb("c_rsq", [128, 2 * H], F32)
        c_eps = sb("c_eps", [128, 1], F32)
        c_bns = sb("c_bns", [128, 24], F32)
        c_e24 = sb("c_e24", [128, 1], F32)
        c_mv = sb("c_mv", [128, 4], F32)

        psf_h = stack.enter_context(nc.psum_tensor("PSF", [128, 4096], F32))
        PSF = psf_h[:]
        PSB = PSF[:, 3584:4096].bitcast(BF16)
        PSB2 = PSF[:, 3072:3584].bitcast(BF16)

        B._stack = stack
        for s in ("SQ", "SP", "SA", "SV"):
            B.mksem(stack, s)

        def a2(off, n):
            return A2[:, off:off + n]

        def a4(off, n):
            return A4[:, off:off + n]

        ev_sc_ld = B.op("sy", lambda e: e.dma_start(out=sscale[:], in_=scale), "Dsc", 16)
        ev_b1_ld = B.op("sy", lambda e: e.dma_start(out=sb1[:], in_=b1), "Db1", 16)
        ev_bo_ld = B.op("sy", lambda e: e.dma_start(out=sbo[:], in_=b_out), "Dbo", 16)
        ev_b2_ld = B.op("sy", lambda e: e.dma_start(out=sb2[:], in_=b2), "Db2", 16)
        B.op("po", lambda e: e.iota(io32[:], pattern=[[1, 128]], base=0, channel_multiplier=-1), "SQ")
        B.wait("po", "SQ")
        ev_idb = B.op("po", lambda e: e.tensor_single_scalar(identb[:], io32[:], 0, op=ALU.is_equal), "SQ")
        ev_idf = B.op("po", lambda e: e.tensor_single_scalar(identf[:], io32[0:32, 0:32], 0, op=ALU.is_equal), "SQ")
        ev_ones = B.op("po", lambda e: e.memset(onesf[:], 1.0), "SQ")
        ev_o64 = B.op("po", lambda e: e.memset(ones64[:], 1.0), "SQ")
        ev_eps = B.op("po", lambda e: e.memset(c_eps[:], EPS), "SQ")
        B.op("po", lambda e: e.memset(c_e24[:], 1e-24), "SQ")

        B.wev("te", ev_sc_ld)
        B.wev("te", ev_ones)
        ev = B.op("te", lambda e: e.matmul(PSF[:, 0:16], lhsT=onesf[:], rhs=sscale[:],
                                           start=True, stop=True), "SP")
        B.wev("sc", ev)
        ev = B.op("sc", lambda e: e.activation(exp_sc[:], PSF[:, 0:16], AF.Exp), "SA")
        B.wev("ve", ev)
        ev_negm = B.op("ve", lambda e: e.tensor_scalar_mul(negM[:], exp_sc[:], -1.0), "SV")
        B.wev("te", ev_bo_ld)
        B.op("te", lambda e: e.matmul(PSF[:, 512:1024], lhsT=onesf[:], rhs=sbo[0:1, 0:512],
                                      start=True, stop=True), "SP")
        B.op("te", lambda e: e.matmul(PSF[:, 1024:1536], lhsT=onesf[:], rhs=sbo[0:1, 512:1024],
                                      start=True, stop=True), "SP")
        B.wev("te", ev_b2_ld)
        B.op("te", lambda e: e.matmul(PSF[:, 1536:2048], lhsT=onesf[:], rhs=sb2[0:1, 0:512],
                                      start=True, stop=True), "SP")
        B.op("te", lambda e: e.matmul(PSF[:, 2048:2560], lhsT=onesf[:], rhs=sb2[0:1, 512:1024],
                                      start=True, stop=True), "SP")
        B.wev("te", ev_b1_ld)
        B.wev("te", ev_idf)
        ev = B.op("te", lambda e: e.transpose(PSF[:, 2560:2592], sb1[:], identf[:]), "SP")
        B.wev("ve", ev)
        B.op("ve", lambda e: e.tensor_copy(borep[:], PSF[:, 512:1536]), "SV")
        B.op("ve", lambda e: e.tensor_copy(b2rep[:], PSF[:, 1536:2560]), "SV")
        ev_bconsts = B.op("ve", lambda e: e.tensor_copy(b1c[:], PSF[:, 2560:2592]), "SV")
        B.barrier()

        o_wkvq = 0
        o_xn16 = 24576
        o_xnT = 26624
        o_k16 = 28672
        o_q16 = 30720
        o_vst = 32768
        o_kTt = 34848
        o_qTt = 36896
        o_wc16 = 38944
        f_xst = 0
        f_lnsq = 4096
        f_kev = 5120
        f_qev = 6144
        f_wst = 7168

        f_lnsq2 = 11264

        def emit_ln(xin_ev, xin_ap, out_ap, cs=0):
            bns = c_bns[:, cs * 12:(cs + 1) * 12]
            mv = c_mv[:, cs * 2:(cs + 1) * 2]
            lnv = c_lnv[:, cs:cs + 1]
            rstd = c_rstd[:, cs:cs + 1]
            B.wev("ve", xin_ev)
            B.op("ve", lambda e: e.bn_stats(bns[:, 0:6], xin_ap[:, 0:512]), "SV")
            e2 = B.op("ve", lambda e: e.bn_stats(bns[:, 6:12], xin_ap[:, 512:1024]), "SV")
            B.wev("ve", e2)
            e3 = B.op("ve", lambda e: e.bn_aggr(mv, bns), "SV")
            B.wev("sc", e3)
            e8 = B.op("sc", lambda e: e.activation(lnv, mv[:, 1:2], AF.Ln, bias=c_eps[:, 0:1]), "SA")
            B.wev("sc", e8)
            e9 = B.op("sc", lambda e: e.activation(rstd, lnv, AF.Exp, scale=-0.5), "SA")
            B.wev("ve", e9)
            return B.op("ve", lambda e: e.tensor_scalar(out_ap, xin_ap, mv[:, 0:1], rstd,
                                                        op0=ALU.subtract, op1=ALU.mult), "SV")

        wst_free = [None, None]
        for c in range(8):
            p = c % 2
            wst = a4(f_wst + p * 2048, 2048)
            B.wev("sy", wst_free[p])
            ld1 = B.op("sy", lambda e, c=c, wst=wst: e.dma_start(
                out=wst, in_=w_kv[c * 128:(c + 1) * 128, :]), "Dw%d" % p, 16)
            B.wev("ve", ld1)
            cp1 = B.op("ve", lambda e, c=c, wst=wst: e.tensor_copy(
                a2(o_wkvq + c * 3072, 2048), wst), "SV")
            B.wev("sy", cp1)
            ld2 = B.op("sy", lambda e, c=c, wst=wst: e.dma_start(
                out=wst[:, 0:1024], in_=w_q[c * 128:(c + 1) * 128, :]), "Dw%d" % p, 16)
            B.wev("ve", ld2)
            wst_free[p] = B.op("ve", lambda e, c=c, wst=wst: e.tensor_copy(
                a2(o_wkvq + c * 3072 + 2048, 1024), wst[:, 0:1024]), "SV")
        ev_wkvq = wst_free[1]

        units = []
        for c in range(8):
            for hhalf in range(2):
                units.append((W1[c * 128:(c + 1) * 128, hhalf * 2048:(hhalf + 1) * 2048],
                              W1bf[c * 128:(c + 1) * 128, hhalf * 2048:(hhalf + 1) * 2048], 2048))
        for ic in range(32):
            units.append((W2[ic * 128:(ic + 1) * 128, :],
                          W2bf[ic * 128:(ic + 1) * 128, :], 1024))
        for c in range(8):
            units.append((w_out[c * 128:(c + 1) * 128, :],
                          wobf[c * 128:(c + 1) * 128, :], 1024))
        unit_i = [0]
        wc_free = [None, None]
        wst2_free = [wst_free[0], wst_free[1]]

        def emit_wunit():
            if unit_i[0] >= len(units):
                return
            i = unit_i[0]
            unit_i[0] += 1
            src, dst, ncol = units[i]
            p = i % 2
            wst = a4(f_wst + p * 2048, ncol)
            wc = a2(o_wc16 + p * 2048, ncol)
            B.wev("sy", wst2_free[p])
            ld = B.op("sy", lambda e, src=src, wst=wst: e.dma_start(out=wst, in_=src), "Dw%d" % p, 16)
            B.wev("ve", ld)
            B.wev("ve", wc_free[p])
            cp = B.op("ve", lambda e, wst=wst, wc=wc: e.tensor_copy(wc, wst), "SV")
            wst2_free[p] = cp
            B.wev("po", cp)
            wc_free[p] = B.op("po", lambda e, dst=dst, wc=wc: e.dma_start(out=dst, in_=wc), "Dwc%d" % p, 16)

        for p in range(2):
            vst3 = a2(o_vst + p * 1040, 1040).rearrange("p (s c) -> p s c", c=65)
            B.op("po", lambda e, vst3=vst3: e.memset(vst3[:, :, 64:65], 1.0), "SQ")
        ev_vones = ("SQ", B.cnt["SQ"])

        x_free = [None, None, None, None]
        x_ld = [None] * NT

        def emit_xload(t):
            p = t % 4
            B.wev("sy", x_free[p])
            x_ld[t] = B.op("sy", lambda e, t=t, p=p: e.dma_start(
                out=a4(f_xst + p * 1024, D), in_=x[t * 128:(t + 1) * 128, :]), "Dx%d" % p, 16)

        emit_xload(0)
        emit_xload(1)

        pg = PSF[:, 0:3072]
        pg_free = [None]
        psb_free = [None]
        psb2_free = [None]
        kTt_free = [None, None]
        qTt_free = [None, None]
        vst_free = [None, None]
        k16_free = [None, None]
        q16_free = [None, None]
        xn16_free = [None, None]
        xnT_free = [None, None]
        kev_free = [None]
        qev_free = [None]

        st1 = {}

        def emit_S1a(t):
            own = t < NTO
            if t + 2 < NT:
                emit_xload(t + 2)
            emit_wunit()
            p2 = t % 2
            cs = t % 2
            xst = a4(f_xst + (t % 4) * 1024, D)
            xn16 = a2(o_xn16 + p2 * 1024, D)
            xnT = a2(o_xnT + p2 * 1024, D)

            B.wev("ve", xn16_free[p2])
            ev_xn = emit_ln(x_ld[t], xst, xn16, cs)
            x_free[t % 4] = ev_xn

            B.wev("te", ev_xn)
            B.wev("te", ev_idb)
            B.wev("te", psb2_free[0])
            for c in range(8):
                evt = B.op("te", lambda e, c=c, xn16=xn16: e.transpose(
                    PSB2[:, c * 128:(c + 1) * 128], xn16[:, c * 128:(c + 1) * 128], identb[:]),
                    "SP" if c == 7 else None)
            B.wev("sc", evt)
            B.wev("sc", xnT_free[p2])
            B.op("sc", lambda e, xnT=xnT: e.copy(xnT[:, 0:512], PSB2[:, 0:512]), "SA")
            ev_xnT = B.op("sc", lambda e, xnT=xnT: e.copy(xnT[:, 512:1024], PSB2[:, 512:1024]), "SA")
            psb2_free[0] = ev_xnT
            xn16_free[p2] = ev_xnT
            st1[t] = [ev_xnT, cs, own, p2, None]

        def emit_S1b(t):
            ev_xnT, cs, own, p2, _ = st1[t]
            xnT = a2(o_xnT + p2 * 1024, D)
            B.wev("te", ev_xnT)
            if t == 0:
                B.wev("te", ev_wkvq)
            if pg_free[0] is not None:
                for _ev in pg_free[0]:
                    B.wev("te", _ev)
            nsl = 6 if own else 4
            for ns in range(nsl):
                for c in range(8):
                    evp = B.op("te", lambda e, ns=ns, c=c, xnT=xnT: e.matmul(
                        pg[:, ns * 512:(ns + 1) * 512],
                        lhsT=xnT[:, c * 128:(c + 1) * 128],
                        rhs=a2(o_wkvq + c * 3072 + ns * 512, 512),
                        start=(c == 0), stop=(c == 7)),
                        "SP" if (ns == nsl - 1 and c == 7) else None)
            st1[t][4] = evp
            xnT_free[p2] = evp

        def emit_S2(t):
            _xnT, cs, own, p2, ev_pg = st1.pop(t)
            kev = a4(f_kev, D)
            B.wev("sc", ev_pg)
            B.wev("sc", kev_free[0])
            B.wev("sc", qev_free[0])
            e_k0 = B.op("sc", lambda e: e.copy(kev, pg[:, 0:1024]), "SA")
            if own:
                qev = a4(f_qev, D)
                e_q0 = B.op("sc", lambda e: e.copy(qev, pg[:, 2048:3072]), "SA")
            vst = a2(o_vst + p2 * 1040, 1040)
            B.wev("ve", ev_pg)
            B.wev("ve", vst_free[p2])
            if t < 2:
                B.wev("ve", ev_vones)
            for pr in range(8):
                e_v = B.op("ve", lambda e, pr=pr, vst=vst: e.tensor_copy(
                    vst.rearrange("p (s c) -> p s c", c=65)[:, 2 * pr:2 * pr + 2, 0:64],
                    pg[:, 1024 + pr * 128:1024 + (pr + 1) * 128].rearrange("p (s d) -> p s d", d=DH)),
                    "SV" if pr == 7 else None)
            ev_vdone = e_v
            pg_free[0] = [ev_vdone, e_q0] if own else [ev_vdone, e_k0]

            B.wev("sc", e_k0)
            ksq = a4(f_lnsq if cs == 0 else f_lnsq2, D)
            kssk = c_ssk[:, cs * H:(cs + 1) * H]
            kssk2 = c_ssk2[:, cs * H:(cs + 1) * H]
            kt1 = c_t1k[:, cs * H:(cs + 1) * H]
            krs = c_rsk[:, cs * H:(cs + 1) * H]
            e_k1 = B.op("sc", lambda e, ksq=ksq: e.activation(ksq, kev, AF.Square), "SA")
            B.wev("ve", e_k1)
            e_k2 = B.op("ve", lambda e, ksq=ksq, kssk=kssk: e.reduce_sum(
                kssk, ksq.rearrange("p (s d) -> p s d", d=DH), axis=AX), "SV")
            B.wev("sc", e_k2)
            e_k4 = B.op("sc", lambda e, kssk=kssk, kt1=kt1: e.activation(
                kt1, kssk, AF.Ln, bias=c_e24[:, 0:1]), "SA")
            B.wev("sc", e_k4)
            e_k5 = B.op("sc", lambda e, kt1=kt1, krs=krs: e.activation(krs, kt1, AF.Exp, scale=-0.5), "SA")
            k16 = a2(o_k16 + p2 * 1024, D)
            B.wev("po", e_k5)
            B.wev("po", k16_free[p2])
            e_k6 = B.op("po", lambda e, k16=k16, krs=krs: e.tensor_tensor(
                k16.rearrange("p (s d) -> p s d", d=DH),
                kev.rearrange("p (s d) -> p s d", d=DH),
                krs.unsqueeze(2).broadcast_to([128, H, DH]), op=ALU.mult), "SQ")
            kev_free[0] = e_k6

            if own:
                B.wev("sc", e_q0)
                qsq = a4(f_lnsq if cs == 0 else f_lnsq2, D)
                qrs = c_rsq[:, cs * H:(cs + 1) * H]
                e_q1 = B.op("sc", lambda e, qsq=qsq: e.activation(qsq, qev, AF.Square), "SA")
                B.wev("ve", e_q1)
                e_q2 = B.op("ve", lambda e, qsq=qsq, kssk=kssk: e.reduce_sum(
                    kssk, qsq.rearrange("p (s d) -> p s d", d=DH), axis=AX), "SV")
                B.wev("sc", e_q2)
                e_q4 = B.op("sc", lambda e, kssk=kssk, kt1=kt1: e.activation(
                    kt1, kssk, AF.Ln, bias=c_e24[:, 0:1]), "SA")
                B.wev("sc", e_q4)
                e_q5 = B.op("sc", lambda e, kt1=kt1, qrs=qrs: e.activation(qrs, kt1, AF.Exp, scale=-0.5), "SA")
                B.wev("ve", e_q5)
                B.wev("ve", ev_negm)
                e_q6 = B.op("ve", lambda e, qrs=qrs: e.tensor_mul(qrs, qrs, exp_sc[:]), "SV")
                q16 = a2(o_q16 + p2 * 1024, D)
                B.wev("po", e_q6)
                B.wev("po", q16_free[p2])
                e_q7 = B.op("po", lambda e, q16=q16, qrs=qrs: e.tensor_tensor(
                    q16.rearrange("p (s d) -> p s d", d=DH),
                    qev.rearrange("p (s d) -> p s d", d=DH),
                    qrs.unsqueeze(2).broadcast_to([128, H, DH]), op=ALU.mult), "SQ")
                qev_free[0] = e_q7

            B.wev("te", e_k6)
            B.wev("te", psb_free[0])
            for c in range(8):
                evt = B.op("te", lambda e, c=c, k16=k16: e.transpose(
                    PSB[:, c * 128:(c + 1) * 128], k16[:, c * 128:(c + 1) * 128], identb[:]),
                    "SP" if c == 7 else None)
            kTt = a2(o_kTt + p2 * 1024, D)
            B.wev("ve", evt)
            B.wev("ve", kTt_free[p2])
            B.op("ve", lambda e, kTt=kTt: e.tensor_copy(kTt[:, 0:512], PSB[:, 0:512]), "SV")
            e_kTt = B.op("ve", lambda e, kTt=kTt: e.tensor_copy(kTt[:, 512:1024], PSB[:, 512:1024]), "SV")
            psb_free[0] = e_kTt
            k16_free[p2] = e_kTt
            B.wev("po", e_kTt)
            kTt_free[p2] = B.op("po", lambda e, t=t, kTt=kTt: e.dma_start(
                out=kT_sp[:, t * 128:(t + 1) * 128].rearrange("(c p) f -> p c f", p=128),
                in_=kTt.rearrange("p (c f) -> p c f", c=8)), "Dkt%d" % p2, 16)

            if own:
                B.wev("te", e_q7)
                B.wev("te", psb_free[0])
                for c in range(8):
                    evt = B.op("te", lambda e, c=c, q16=q16: e.transpose(
                        PSB[:, c * 128:(c + 1) * 128], q16[:, c * 128:(c + 1) * 128], identb[:]),
                        "SP" if c == 7 else None)
                qTt = a2(o_qTt + p2 * 1024, D)
                B.wev("ve", evt)
                B.wev("ve", qTt_free[p2])
                B.op("ve", lambda e, qTt=qTt: e.tensor_copy(qTt[:, 0:512], PSB[:, 0:512]), "SV")
                e_qTt = B.op("ve", lambda e, qTt=qTt: e.tensor_copy(qTt[:, 512:1024], PSB[:, 512:1024]), "SV")
                psb_free[0] = e_qTt
                q16_free[p2] = e_qTt
                B.wev("po", e_qTt)
                qTt_free[p2] = B.op("po", lambda e, t=t, qTt=qTt: e.dma_start(
                    out=qT_sp[:, t * 128:(t + 1) * 128].rearrange("(c p) f -> p c f", p=128),
                    in_=qTt.rearrange("p (c f) -> p c f", c=8)), "Dqt%d" % p2, 16)

            B.wev("po", ev_vdone)
            vst_free[p2] = B.op("po", lambda e, t=t, vst=vst: e.dma_start(
                out=v_sp[:, t].rearrange("c p f -> p c f"),
                in_=vst.rearrange("p (c f) -> p c f", c=8)), "Dvs%d" % p2, 16)

        for i in range(NT + 2):
            if i < NT:
                emit_S1a(i)
            if 2 <= i:
                emit_S2(i - 2)
            if 1 <= i <= NT:
                emit_S1b(i - 1)

        while unit_i[0] < len(units):
            emit_wunit()

        B.barrier()
        o_kT = 0
        o_vp = 16384
        o_qTb = 33024
        o_P16 = 34048
        o_ot = 37120
        f_osb = 0

        o0 = PSF[:, 0:512]
        o1 = PSF[:, 512:1024]
        RING0 = 1024

        kld = [None, None]
        vld = [None, None]
        kT_free = [None, None]
        vp_free = [None, None]
        qld = [None, None]
        qT_free = [None, None]
        slot_free = [None] * 6
        p16_free = [None, None]
        o_free = [None]
        osb_free = [None, None]
        ot_free = [None, None]
        rl_free = [None, None]

        def emit_kvload(pr):
            p = pr % 2
            B.wev("sy", kT_free[p])
            kld[p] = B.op("sy", lambda e, pr=pr, p=p: e.dma_start(
                out=a2(o_kT + p * 8192, 8192),
                in_=kT_sp[pr * 128:(pr + 1) * 128, :]), "Dk%d" % p, 16)
            B.wev("sy", vp_free[p])
            vld[p] = B.op("sy", lambda e, pr=pr, p=p: e.dma_start(
                out=a2(o_vp + p * 8320, 8320).rearrange("p (c f) -> p c f", c=64),
                in_=v_sp[pr].rearrange("c p f -> p c f")), "Dv%d" % p, 16)

        emit_kvload(0)
        emit_kvload(1)
        pending_div = [None]

        chunks = [(jc, eh) for jc in range(64) for eh in (0, 1)]
        groups = [chunks[i:i + 3] for i in range(0, 126, 3)] + [chunks[126:128]]

        for pr in range(NPAIR):
            p = pr % 2
            kT = a2(o_kT + p * 8192, 8192)
            vp = a2(o_vp + p * 8320, 8320)
            last_pv_ev = None
            last_score_ev = None

            def emit_qld(ig2, pr=pr):
                pq2 = ig2 % 2
                B.wev("sy", qT_free[pq2])
                qld[pq2] = B.op("sy", lambda e, pr=pr, ig2=ig2, pq2=pq2: e.dma_start(
                    out=a2(o_qTb + pq2 * 512, 512),
                    in_=qT_sp[pr * 128:(pr + 1) * 128, ig2 * 512:(ig2 + 1) * 512]), "Dq%d" % pq2, 16)

            emit_qld(0)
            emit_qld(1)
            for ig in range(NIG):
                pq = ig % 2
                qTb = a2(o_qTb + pq * 512, 512)
                first_pv = [True]
                pend = []
                seq = list(enumerate(groups)) + [(None, None), (None, None)]
                for gi, grp in seq:
                    if gi is not None:
                        gslot = (3 * gi) % 6
                        gpar = gi % 2
                        fd = 512 * len(grp)
                        for ci in range(len(grp)):
                            B.wev("te", slot_free[gslot + ci])
                        if gi == 0:
                            B.wev("te", kld[p])
                            B.wev("te", qld[pq])
                            if ig == 0:
                                B.wev("te", vld[p])
                        for ci, (jc, eh) in enumerate(grp):
                            sl = gslot + ci
                            evs = B.op("te", lambda e, eh=eh, jc=jc, sl=sl, kT=kT, qTb=qTb: e.matmul(
                                PSF[:, RING0 + sl * 512:RING0 + (sl + 1) * 512],
                                lhsT=kT[eh * 64:(eh + 1) * 64, jc * 128:(jc + 1) * 128],
                                rhs=qTb[eh * 64:(eh + 1) * 64, :],
                                start=True, stop=True, tile_position=(eh * 64, 0)),
                                "SP" if ci == len(grp) - 1 else None)
                        last_score_ev = evs
                    if len(pend) == 2 or (gi is None and pend):
                        pgrp, pgpar, peve = pend.pop(0)
                        if first_pv[0]:
                            B.wev("te", o_free[0])
                            first_pv[0] = False
                        B.wev("te", peve)
                        for ci, (jc, eh) in enumerate(pgrp):
                            oreg = o0 if eh == 0 else o1
                            evpv = B.op("te", lambda e, jc=jc, eh=eh, ci=ci, pgpar=pgpar, vp=vp, oreg=oreg: e.matmul(
                                oreg[0:65, :],
                                lhsT=vp[:, jc * 130 + eh * 65:jc * 130 + (eh + 1) * 65],
                                rhs=a2(o_P16 + pgpar * 1536 + ci * 512, 512),
                                start=(jc == 0), stop=(jc == 63)),
                                "SP" if ci == len(pgrp) - 1 else None)
                        p16_free[pgpar] = evpv
                        last_pv_ev = evpv
                    if gi is not None:
                        B.wev("sc", last_score_ev)
                        B.wev("sc", p16_free[gpar])
                        eve = B.op("sc", lambda e, gslot=gslot, gpar=gpar, fd=fd: e.activation(
                            a2(o_P16 + gpar * 1536, fd),
                            PSF[:, RING0 + gslot * 512:RING0 + gslot * 512 + fd], AF.Exp), "SA")
                        for ci in range(len(grp)):
                            slot_free[gslot + ci] = eve
                        pend.append((grp, gpar, eve))
                        if gi == 1 and pending_div[0] is not None:
                            pending_div[0]()
                            pending_div[0] = None
                last_pv_ev = evpv

                def emit_divide(pr=pr, ig=ig, last_pv_ev=last_pv_ev):
                    pp = (pr * NIG + ig) % 2
                    rlh = rl[0:1, 0:1024]
                    B.wev("ve", last_pv_ev)
                    B.wev("ve", osb_free[pp])
                    B.wev("ve", rl_free[0])
                    e_r0 = B.op("ve", lambda e, rlh=rlh: e.reciprocal(rlh[:, 0:512], PSF[64:65, 0:512]), "SV")
                    e_r1 = B.op("ve", lambda e, rlh=rlh: e.reciprocal(rlh[:, 512:1024], PSF[64:65, 512:1024]), "SV")
                    osb0 = a4(f_osb + pp * 1024, 512)
                    osb1 = a4(f_osb + pp * 1024 + 512, 512)
                    e_c0 = B.op("ve", lambda e, osb0=osb0: e.tensor_copy(osb0[0:64, :], PSF[0:64, 0:512]), "SV")
                    e_c1 = B.op("ve", lambda e, osb1=osb1: e.tensor_copy(osb1[0:64, :], PSF[0:64, 512:1024]), "SV")
                    B.wev("te", e_r0)
                    B.wev("te", e_c0)
                    B.wev("te", ev_o64)
                    e_b0 = B.op("te", lambda e, rlh=rlh: e.matmul(
                        PSF[0:64, 0:512], lhsT=ones64[:], rhs=rlh[:, 0:512], start=True, stop=True), "SP")
                    ot_a = a2(o_ot + pp * 1024, 512)
                    ot_b = a2(o_ot + pp * 1024 + 512, 512)
                    B.wev("ve", e_b0)
                    B.wev("ve", ot_free[pp])
                    e_m0 = B.op("ve", lambda e, osb0=osb0, ot_a=ot_a: e.tensor_tensor(
                        ot_a[0:64, :], osb0[0:64, :], PSF[0:64, 0:512], op=ALU.mult), "SV")
                    B.wev("te", e_r1)
                    B.wev("te", e_c1)
                    e_b1 = B.op("te", lambda e, rlh=rlh: e.matmul(
                        PSF[0:64, 512:1024], lhsT=ones64[:], rhs=rlh[:, 512:1024], start=True, stop=True), "SP")
                    rl_free[0] = e_b1
                    B.wev("ve", e_b1)
                    e_m1 = B.op("ve", lambda e, osb1=osb1, ot_b=ot_b: e.tensor_tensor(
                        ot_b[0:64, :], osb1[0:64, :], PSF[0:64, 512:1024], op=ALU.mult), "SV")
                    osb_free[pp] = e_m1
                    o_free[0] = e_m1
                    B.wev("po", e_m1)
                    B.op("po", lambda e, pr=pr, ig=ig, ot_a=ot_a: e.dma_start(
                        out=o_sp[pr * 128:pr * 128 + 64, ig * 512:(ig + 1) * 512], in_=ot_a[0:64, :]),
                        "Dosa%d" % pp, 16)
                    ot_free[pp] = B.op("po", lambda e, pr=pr, ig=ig, ot_b=ot_b: e.dma_start(
                        out=o_sp[pr * 128 + 64:pr * 128 + 128, ig * 512:(ig + 1) * 512], in_=ot_b[0:64, :]),
                        "Dosb%d" % pp, 16)
                pending_div[0] = emit_divide
                qT_free[pq] = last_score_ev
                if ig + 2 < NIG:
                    emit_qld(ig + 2)
            kT_free[p] = last_score_ev
            vp_free[p] = last_pv_ev
            if pr + 2 < NPAIR:
                emit_kvload(pr + 2)
        if pending_div[0] is not None:
            pending_div[0]()
            pending_div[0] = None

        B.barrier()
        o_wo = 0
        o_W2 = 8192
        o_W1r = 40960
        o_yt = 43008
        o_x1T = 59392
        o_otc = 67584
        f_x1g = 0
        f_s1 = 5120
        f_ln1 = 6144
        f_x3 = 7168
        f_out = 9216

        ev_wo = B.op("sy", lambda e: e.dma_start(
            out=a2(o_wo, 8192).rearrange("p (c f) -> p c f", c=8),
            in_=wobf.rearrange("(c p) f -> p c f", p=128)), "Dwo", 16)
        ev_W2 = B.op("sy", lambda e: e.dma_start(
            out=a2(o_W2, 32768).rearrange("p (c f) -> p c f", c=32),
            in_=W2bf.rearrange("(c p) f -> p c f", p=128)), "DW2", 16)

        pz = PSF[:, 0:1024]
        pz2 = PSF[:, 1024:2048]
        pyr = [PSF[:, 2048:2560], PSF[:, 2560:3072]]

        ot_ld = [None, None]
        otc_free = [None, None]
        x3_ld = [None, None]
        x3_free = [None, None]
        w1_ld = [None, None]
        w1_free = [None, None]
        py_free = [None, None]
        pz_free = None
        pz2_free = None
        yt_free = None
        x1T_free = None
        s1_free = None
        ln1_free = None
        out_free = [None, None]
        x1g_free = [None, None, None, None]
        last_y_dma = None

        def emit_ln3(xin_ev, xin_ap, out_ap, cs=0):
            return emit_ln(xin_ev, xin_ap, out_ap, cs)

        x1T_free2 = [None, None]
        for g in range(8):
            gp = g % 2
            x1T_base = o_x1T + gp * 4096
            ev_xt = [None] * 4
            for ti in range(4):
                t = g * 4 + ti
                pt2 = t % 2
                B.wev("sy", otc_free[pt2])
                ot_ld[pt2] = B.op("sy", lambda e, t=t, pt2=pt2: e.dma_start(
                    out=a2(o_otc + pt2 * 1024, 1024).rearrange("p (c f) -> p c f", c=8),
                    in_=o_sp[:, t * 128:(t + 1) * 128].rearrange("(c p) f -> p c f", p=128)),
                    "Dot%d" % pt2, 16)
                B.wev("sy", x3_free[pt2])
                x3_ld[pt2] = B.op("sy", lambda e, t=t, pt2=pt2: e.dma_start(
                    out=a4(f_x3 + pt2 * 1024, D), in_=x[t * 128:(t + 1) * 128, :]), "Dx3%d" % pt2, 16)

                B.wev("te", ot_ld[pt2])
                if t == 0:
                    B.wev("te", ev_wo)
                B.wev("te", pz_free)
                otc = a2(o_otc + pt2 * 1024, 1024)
                for ehh in range(2):
                    for c in range(8):
                        ev_z = B.op("te", lambda e, ehh=ehh, c=c, otc=otc: e.matmul(
                            pz[:, ehh * 512:(ehh + 1) * 512],
                            lhsT=otc[:, c * 128:(c + 1) * 128],
                            rhs=a2(o_wo + c * 1024 + ehh * 512, 512),
                            start=(c == 0), stop=(c == 7)),
                            "SP" if (ehh == 1 and c == 7) else None)
                otc_free[pt2] = ev_z
                B.wev("ve", ev_z)
                B.wev("ve", s1_free)
                B.wev("ve", ev_bconsts)
                e_s1 = B.op("ve", lambda e: e.tensor_add(a4(f_s1, D), pz, borep[:]), "SV")
                pz_free = e_s1
                B.wev("ve", ln1_free)
                e_l1 = emit_ln3(e_s1, a4(f_s1, D), a4(f_ln1, D), t % 2)
                s1_free = e_l1
                B.wev("ve", x3_ld[pt2])
                B.wev("ve", x1g_free[ti])
                e_x1 = B.op("ve", lambda e, ti=ti, pt2=pt2: e.tensor_add(
                    a4(f_x1g + ti * 1024, D), a4(f_ln1, D), a4(f_x3 + pt2 * 1024, D)), "SV")
                ln1_free = e_x1
                x3_free[pt2] = e_x1
                xnb = a2(o_x1T, 4096)
                e_ln2 = emit_ln3(e_x1, a4(f_x1g + ti * 1024, D), a2(o_otc + pt2 * 1024, 1024), t % 2)
                B.wev("te", e_ln2)
                B.wev("te", psb_free[0])
                for c in range(8):
                    evt = B.op("te", lambda e, c=c, pt2=pt2: e.transpose(
                        PSB[:, c * 128:(c + 1) * 128],
                        a2(o_otc + pt2 * 1024, 1024)[:, c * 128:(c + 1) * 128], identb[:]),
                        "SP" if c == 7 else None)
                otc_free[pt2] = evt
                B.wev("ve", evt)
                B.wev("ve", x1T_free2[gp])
                B.op("ve", lambda e, ti=ti, x1T_base=x1T_base: e.tensor_copy(
                    a2(x1T_base, 4096).rearrange("p (c f) -> p c f", c=8)[:, 0:4, ti * 128:(ti + 1) * 128],
                    PSB[:, 0:512].rearrange("p (c f) -> p c f", c=4)), "SV")
                ev_xt[ti] = B.op("ve", lambda e, ti=ti, x1T_base=x1T_base: e.tensor_copy(
                    a2(x1T_base, 4096).rearrange("p (c f) -> p c f", c=8)[:, 4:8, ti * 128:(ti + 1) * 128],
                    PSB[:, 512:1024].rearrange("p (c f) -> p c f", c=4)), "SV")
                psb_free[0] = ev_xt[ti]

            def emit_w1load(e_idx):
                pw = e_idx % 2
                B.wev("sy", w1_free[pw])
                w1_ld[pw] = B.op("sy", lambda e, e_idx=e_idx, pw=pw: e.dma_start(
                    out=a2(o_W1r + pw * 1024, 1024).rearrange("p (c f) -> p c f", c=8),
                    in_=W1bf[:, e_idx * 128:(e_idx + 1) * 128].rearrange("(c p) f -> p c f", p=128)),
                    "Dwr%d" % pw, 16)

            emit_w1load(0)
            emit_w1load(1)
            for ei in range(32):
                pw = ei % 2
                if ei + 2 < 32:
                    pass
                B.wev("te", w1_ld[pw])
                if ei < 2:
                    for tii in range(4):
                        B.wev("te", ev_xt[tii])
                B.wev("te", py_free[pw])
                if ei == 0:
                    B.wev("sc", yt_free)
                for c in range(8):
                    e_y = B.op("te", lambda e, ei=ei, c=c, pw=pw, x1T_base=x1T_base: e.matmul(
                        pyr[pw], lhsT=a2(o_W1r + pw * 1024, 1024)[:, c * 128:(c + 1) * 128],
                        rhs=a2(x1T_base + c * 512, 512),
                        start=(c == 0), stop=(c == 7)), "SP" if c == 7 else None)
                w1_free[pw] = e_y
                x1T_free2[gp] = e_y
                B.wev("sc", e_y)
                e_yt = B.op("sc", lambda e, ei=ei, pw=pw: e.activation(
                    a2(o_yt + ei * 512, 512), pyr[pw], AF.Relu, bias=b1c[:, ei:ei + 1]), "SA")
                py_free[pw] = e_yt
                if ei + 2 < 32:
                    emit_w1load(ei + 2)

            ev_yt_all = e_yt

            for ti in range(4):
                t = g * 4 + ti
                pt2 = t % 2
                B.wev("te", ev_yt_all)
                if g == 0 and ti == 0:
                    B.wev("te", ev_W2)
                B.wev("te", pz2_free)
                for ehh in range(2):
                    for ic in range(32):
                        e_z2 = B.op("te", lambda e, ehh=ehh, ic=ic, ti=ti: e.matmul(
                            pz2[:, ehh * 512:(ehh + 1) * 512],
                            lhsT=a2(o_yt + ic * 512, 512)[:, ti * 128:(ti + 1) * 128],
                            rhs=a2(o_W2 + ic * 1024 + ehh * 512, 512),
                            start=(ic == 0), stop=(ic == 31)),
                            "SP" if (ehh == 1 and ic == 31) else None)
                yt_free = e_z2
                B.wev("ve", e_z2)
                B.wev("ve", s1_free)
                e_s2 = B.op("ve", lambda e: e.tensor_add(a4(f_s1, D), pz2, b2rep[:]), "SV")
                pz2_free = e_s2
                B.wev("ve", ln1_free)
                e_l2 = emit_ln3(e_s2, a4(f_s1, D), a4(f_ln1, D), t % 2)
                s1_free = e_l2
                B.wev("ve", out_free[pt2])
                e_o = B.op("ve", lambda e, ti=ti, pt2=pt2: e.tensor_add(
                    a4(f_out + pt2 * 1024, D), a4(f_ln1, D), a4(f_x1g + ti * 1024, D)), "SV")
                ln1_free = e_o
                x1g_free[ti] = e_o
                B.wev("po", e_o)
                last_y_dma = B.op("po", lambda e, t=t, pt2=pt2: e.dma_start(
                    out=y[t * 128:(t + 1) * 128, :], in_=a4(f_out + pt2 * 1024, D)), "Dy%d" % pt2, 16)
                out_free[pt2] = last_y_dma

        B.wait("po", "Dy0")
        B.wait("po", "Dy1")
        B.replay()
    return B.nc


_NC_CACHE = []


def kernel(L0_attn_x, L0_attn_w_q, L0_attn_w_kv, L0_attn_w_out, L0_attn_b_out,
           L0_attn_scale, L0_ffn_W1, L0_ffn_b1, L0_ffn_W2, L0_ffn_b2):
    x = np.ascontiguousarray(np.asarray(L0_attn_x, dtype=np.float32))
    w_q = np.ascontiguousarray(np.asarray(L0_attn_w_q, dtype=np.float32))
    w_kv = np.ascontiguousarray(np.asarray(L0_attn_w_kv, dtype=np.float32))
    w_out = np.ascontiguousarray(np.asarray(L0_attn_w_out, dtype=np.float32))
    b_out = np.ascontiguousarray(np.asarray(L0_attn_b_out, dtype=np.float32)).reshape(1, D)
    sc = np.ascontiguousarray(np.asarray(L0_attn_scale, dtype=np.float32)).reshape(1, H)
    W1a = np.ascontiguousarray(np.asarray(L0_ffn_W1, dtype=np.float32))
    b1a = np.ascontiguousarray(np.asarray(L0_ffn_b1, dtype=np.float32)).reshape(32, 128)
    W2a = np.ascontiguousarray(np.asarray(L0_ffn_W2, dtype=np.float32))
    b2a = np.ascontiguousarray(np.asarray(L0_ffn_b2, dtype=np.float32)).reshape(1, D)

    if not _NC_CACHE:
        _NC_CACHE.append(build_program())
    nc = _NC_CACHE[0]

    in_maps = []
    for c in range(8):
        b, oh = c // 2, c % 2
        xb = x[b]
        if oh == 1:
            xc = np.ascontiguousarray(np.concatenate([xb[OWN:], xb[:OWN]], axis=0))
        else:
            xc = xb
        in_maps.append({
            "x": xc, "w_q": w_q, "w_kv": w_kv, "w_out": w_out, "b_out": b_out,
            "scale": sc, "W1": W1a, "b1": b1a, "W2": W2a, "b2": b2a,
        })
    res = run_bass_kernel_spmd(nc, in_maps, core_ids=list(range(8)))
    out = np.empty((4, S, D), dtype=np.float32)
    for c in range(8):
        b, oh = c // 2, c % 2
        out[b, oh * OWN:(oh + 1) * OWN] = res.results[c]["y"]
    return out


if __name__ == "__main__":
    rng = np.random.default_rng(0)
    inputs = {
        "L0_attn_x": rng.standard_normal((4, S, D), dtype=np.float32),
        "L0_attn_w_q": rng.standard_normal((D, D), dtype=np.float32) * 0.02,
        "L0_attn_w_kv": rng.standard_normal((D, 2 * D), dtype=np.float32) * 0.02,
        "L0_attn_w_out": rng.standard_normal((D, D), dtype=np.float32) * 0.02,
        "L0_attn_b_out": rng.standard_normal((D,), dtype=np.float32) * 0.1,
        "L0_attn_scale": rng.standard_normal((1, H, 1, 1), dtype=np.float32) - 3.0,
        "L0_ffn_W1": rng.standard_normal((D, INTER), dtype=np.float32) * 0.03,
        "L0_ffn_b1": rng.standard_normal((INTER,), dtype=np.float32) * 0.02,
        "L0_ffn_W2": rng.standard_normal((INTER, D), dtype=np.float32) * 0.015,
        "L0_ffn_b2": rng.standard_normal((D,), dtype=np.float32) * 0.02,
    }
    out = kernel(**inputs)
    print("out", out.shape, out.dtype, float(np.abs(out).max()))
